# version 07 (final)
# baseline (speedup 1.0000x reference)
import numpy as np
import ml_dtypes
import concourse.bass as bass
import concourse.mybir as mybir
from concourse.bass_utils import run_bass_kernel_spmd

F32 = mybir.dt.float32
BF16 = mybir.dt.bfloat16
I32 = mybir.dt.int32
AF = mybir.ActivationFunctionType
ALU = mybir.AluOpType
AX = mybir.AxisListType
ds = bass.ds

B, S, E, H, D = 4, 4096, 1024, 16, 64
I3 = 3072
TLOC = 2048
NT = 16
N_LAYERS = 2
TAP = None

PAR_BLOCKS = {0: [(0, 1024), (3072, 1024)], 1: [(1024, 2048)]}
GROUP_QB = {0: [0, 512, 3072, 3584], 1: [1024, 1536, 2048, 2560]}
L0_NCH = {p: [qb // 128 + 4 for qb in GROUP_QB[p]] for p in (0, 1)}
BAND_WIN = [(0, 64), (0, 192), (64, 320), (192, 448), (320, 512), (448, 512)]
V_ROW = H * (D + 1)


def CC_OF(gc):
    return gc if gc < 8 else (gc + 8 if gc >= 24 else gc + 8) if False else (
        gc if gc < 8 else (gc - 16 if gc >= 24 else gc + 8))
WDQ_CH = E * 128


def _host_prepare(inputs):
    x = np.ascontiguousarray(np.asarray(inputs["L0_attn_x"], dtype=np.float32))
    f32 = lambda a: np.ascontiguousarray(np.asarray(a, dtype=np.float32))

    w_inT = f32(np.asarray(inputs["L0_attn_in_proj_w"]).T)
    b_in = f32(inputs["L0_attn_in_proj_b"])
    w_outT = f32(np.asarray(inputs["L0_attn_out_proj_w"]).T)
    b_out = f32(inputs["L0_attn_out_proj_b"])
    wup = f32(inputs["L0_ffn_w_up"])
    wgate = f32(inputs["L0_ffn_w_gate"])
    wdn = f32(inputs["L0_ffn_w_down"])
    wupT, wgateT, wdnT = f32(wup.T), f32(wgate.T), f32(wdn.T)
    wqT = f32(np.asarray(inputs["L1_attn_Wq"]).T)
    wkT = f32(np.asarray(inputs["L1_attn_Wk"]).T)
    wvT = f32(np.asarray(inputs["L1_attn_Wv"]).T)
    woT = f32(np.asarray(inputs["L1_attn_Wo"]).T)
    bq, bk, bv, bo = (f32(inputs["L1_attn_b" + t]) for t in "qkvo")
    rm = f32(inputs["L1_ffn_rail_maker"])
    ru = f32(inputs["L1_ffn_rail_up_proj"])
    w_cpT = f32(np.asarray(inputs["L1_ffn_w_cproj"]).T)

    bd1 = np.zeros((4, 8, 128, 128), np.float32)
    bd2 = np.zeros((4, 32, 128, 128), np.float32)
    for h in range(4):
        for r in range(64):
            c, rr = r // 8, r % 8
            bd1[h, c, rr * 16:rr * 16 + 16, rr * 16:rr * 16 + 16] = rm[h, r]
        for a in range(32):
            ci = a // 4
            for rq in (2 * a, 2 * a + 1):
                kk0 = (rq - 8 * ci) * 16
                mm0 = (rq - 2 * a) * 64
                bd2[h, a, kk0:kk0 + 16, mm0:mm0 + 64] = ru[h, rq].T
    bd1, bd2 = np.ascontiguousarray(bd1), np.ascontiguousarray(bd2)

    kp = np.arange(128)[:, None]
    z0 = np.arange(896)[None, :]
    E0 = np.where(z0 >= kp + 384, np.float32(0), np.float32(-1e9)).astype(ml_dtypes.bfloat16)
    z2 = np.arange(1664)[None, :]
    E2 = np.where((np.abs(z2 - 512 - kp) <= 64) & (z2 < 1152),
                  np.float32(0), np.float32(-1e9)).astype(ml_dtypes.bfloat16)
    ident = np.eye(128, dtype=np.float32)

    def pack_bias(b):
        n = b.shape[0] // 128
        return np.ascontiguousarray(b.reshape(n, 128).T)

    brows = np.concatenate([b_in[2 * E:3 * E], bv, bo, b_out])[None, :]

    in_maps = []
    for cid in range(8):
        b_, p_ = cid // 2, cid % 2
        rows = np.concatenate([np.arange(s, s + l) for s, l in PAR_BLOCKS[p_]])
        in_maps.append({
            "x_loc": np.ascontiguousarray(x[b_][rows]),
            "w_inT": w_inT, "b_in_pk": pack_bias(b_in),
            "w_outT": w_outT, "brows_d": np.ascontiguousarray(brows),
            "wup_nat": np.ascontiguousarray(wup[384 * cid:384 * cid + 384]),
            "wupT_sl": np.ascontiguousarray(wupT[:, 384 * cid:384 * cid + 384]),
            "wgate_nat": np.ascontiguousarray(wgate[384 * cid:384 * cid + 384]),
            "wgateT_sl": np.ascontiguousarray(wgateT[:, 384 * cid:384 * cid + 384]),
            "wdn_nat": np.ascontiguousarray(wdn[:, 384 * cid:384 * cid + 384]),
            "wdnT_sl": np.ascontiguousarray(wdnT[384 * cid:384 * cid + 384]),
            "wqT": wqT, "wkT": wkT, "wvT": wvT, "woT": woT,
            "bq_pk": pack_bias(bq), "bk_pk": pack_bias(bk),
            "bd1": bd1, "bd2": bd2, "w_cpT": w_cpT,
            "maskE0": E0, "maskE2": E2, "identF": ident,
        })
    return in_maps


def _assemble(results, key="out"):
    out = np.empty((B, S, E), np.float32)
    for cid in range(8):
        b_, p_ = cid // 2, cid % 2
        shard = np.asarray(results[cid][key])
        ofs = 0
        for s0, l in PAR_BLOCKS[p_]:
            out[b_, s0:s0 + l] = shard[ofs:ofs + l]
            ofs += l
    return out


class Sem:
    def __init__(self, nc, name):
        self.h = nc.semaphore(name).__enter__()
        self.n = 0

    def inc(self, instr, k=1):
        instr.then_inc(self.h, k)
        self.n += k
        return instr

    def wait(self, eng, n=None):
        eng.wait_ge(self.h, self.n if n is None else n)


def _build(nc):
    di = nc.dram_tensor
    x_loc = di("x_loc", [TLOC, E], F32, kind="ExternalInput")
    w_inT = di("w_inT", [E, 3 * E], F32, kind="ExternalInput")
    b_in_pk = di("b_in_pk", [128, 24], F32, kind="ExternalInput")
    w_outT = di("w_outT", [E, E], F32, kind="ExternalInput")
    brows_d = di("brows_d", [1, 4 * E], F32, kind="ExternalInput")
    wup_nat = di("wup_nat", [384, E], F32, kind="ExternalInput")
    wupT_sl = di("wupT_sl", [E, 384], F32, kind="ExternalInput")
    wgate_nat = di("wgate_nat", [384, E], F32, kind="ExternalInput")
    wgateT_sl = di("wgateT_sl", [E, 384], F32, kind="ExternalInput")
    wdn_nat = di("wdn_nat", [E, 384], F32, kind="ExternalInput")
    wdnT_sl = di("wdnT_sl", [384, E], F32, kind="ExternalInput")
    wqT_d = di("wqT", [E, E], F32, kind="ExternalInput")
    wkT_d = di("wkT", [E, E], F32, kind="ExternalInput")
    wvT_d = di("wvT", [E, E], F32, kind="ExternalInput")
    woT_d = di("woT", [E, E], F32, kind="ExternalInput")
    bq_pk = di("bq_pk", [128, 8], F32, kind="ExternalInput")
    bk_pk = di("bk_pk", [128, 8], F32, kind="ExternalInput")
    bd1_d = di("bd1", [4, 8, 128, 128], F32, kind="ExternalInput")
    bd2_d = di("bd2", [4, 32, 128, 128], F32, kind="ExternalInput")
    w_cpT = di("w_cpT", [16 * E, E], F32, kind="ExternalInput")
    maskE0 = di("maskE0", [128, 896], BF16, kind="ExternalInput")
    maskE2 = di("maskE2", [128, 1664], BF16, kind="ExternalInput")
    identF = di("identF", [128, 128], F32, kind="ExternalInput")

    out_d = di("out", [TLOC, E], F32, kind="ExternalOutput")

    upT_d = di("upT_d", [I3, TLOC], F32)
    gateT_d = di("gateT_d", [I3, TLOC], F32)
    t4T_d = di("t4T_d", [I3, TLOC], F32)
    t5T_d = di("t5T_d", [I3, TLOC], F32)
    h_dram = di("h_dram", [16 * E, TLOC], BF16)
    cpT_bf = di("cpT_bf", [16 * E, E], BF16)
    bd2_bf = di("bd2_bf", [4, 32, 128, 128], BF16)
    bd1_bf = di("bd1_bf", [4, 8, 128, 128], BF16)
    agK_in = [di(f"agK{l}_in", [E, TLOC], BF16) for l in range(2)]
    agK_out = [di(f"agK{l}_out", [8, E, TLOC], BF16) for l in range(2)]
    agV_in = [di(f"agV{l}_in", [TLOC, V_ROW], BF16) for l in range(2)]
    agV_out = [di(f"agV{l}_out", [8, TLOC, V_ROW], BF16) for l in range(2)]
    kKloc = di("kKloc", [E, 4096], BF16)
    vloc = di("vloc", [4096, V_ROW], BF16)
    wdq_in = di("wdq_in", [9 * WDQ_CH], BF16)
    wdq_out = di("wdq_out", [8 * 9 * WDQ_CH], BF16)
    ar_in = [di(f"ar{i}_in", [1, 8], F32) for i in range(5)]
    ar_out = [di(f"ar{i}_out", [1, 8], F32) for i in range(5)]

    vWU = wdq_out.rearrange("(r w j e i) -> r w j e i", r=8, w=3, j=3, e=E, i=128)
    vWD = wdq_out.rearrange("(r w j p o) -> r w j p o", r=8, w=3, j=3, p=128, o=E)
    vV = [agV_out[l].rearrange("r (t p) v -> r p t v", p=128) for l in range(2)]
    vWin = wdq_in.rearrange("(w j e i) -> w j e i", w=3, j=3, e=E, i=128)
    vWinD = wdq_in.rearrange("(w j p o) -> w j p o", w=3, j=3, p=128, o=E)
    vH = h_dram.rearrange("(a p) t -> p a t", p=128)

    sbt = nc.alloc_sbuf_tensor
    x_res = sbt("x_res", [128, NT * E], F32)
    xnq = sbt("xnq", [128, NT * E], BF16)
    qT = sbt("qT", [128, 8 * TLOC], BF16)
    arena = sbt("arena", [128, 21504], BF16)
    arenaF = arena.bitcast(F32)
    arenaI = arena.bitcast(I32)
    kcache = [arena[:, 0:4096], arena[:, 4096:8192]]
    vcache = [arena[:, 8192:12352], arena[:, 12352:16512]]
    fbig = arenaF[:, 0:4096]
    xst = [arenaF[:, 0:2048], arenaF[:, 2048:4096]]
    fst = [arenaF[:, 512 * i:512 * i + 512] for i in range(8)]
    brows_f = arenaF[0:1, 0:4 * E]
    lnst = arenaF[:, 4096:5120]
    sqsc = arenaF[:, 5120:6144]
    ugst = [arenaF[:, 4096 + 512 * i:4608 + 512 * i] for i in range(4)]
    wdi = arenaI[:, 6144:8192]
    ist = wdi
    wdf = arenaF[:, 8192:9216]
    wdqt = [arena[:, 18560 + 1024 * i:19584 + 1024 * i] for i in range(2)]
    sW = arenaF[:, 10496:10688]
    ktst = [arena[:, 0:2048], arena[:, 2048:4096]]
    vst = [arena[:, 8192 + 1040 * i:9232 + 1040 * i] for i in range(2)]
    wst2 = [arena[:, 12352 + 4096 * i:16448 + 4096 * i] for i in range(2)]
    obf = [arena[:, 16512 + 512 * i:17024 + 512 * i] for i in range(2)]
    bd1_sb = arena[:, 0:4096]
    bdst = [arena[:, 4096 + 512 * i:4608 + 512 * i] for i in range(2)]
    yst = arena[:, 5120:5632]
    cpst = [arena[:, 5632 + 1024 * i:6656 + 1024 * i] for i in range(2)]
    hst = [arena[:, 8192 + 2048 * i:10240 + 2048 * i] for i in range(2)]
    wst = [sbt(f"wst{i}", [128, 1024], BF16) for i in range(2)]
    expS = [sbt(f"expS{i}", [128, 1024], BF16) for i in range(2)]
    e0m = sbt("e0m", [128, 896], BF16)
    e2m = sbt("e2m", [128, 1664], BF16)
    identS = sbt("identS", [128, 128], F32)
    biasin = sbt("biasin", [128, 24], F32)
    bqS = sbt("bqS", [128, 8], F32)
    bkS = sbt("bkS", [128, 8], F32)
    brows = sbt("brows", [1, 4 * E], BF16)
    onesb = sbt("onesb", [1, 128], BF16)
    onesf = sbt("onesf", [1, 128], F32)
    stats = sbt("stats", [128, 16], F32)
    macc = sbt("macc", [128, 4], F32)
    sca = sbt("sca", [128, 12], F32)
    scaR = sbt("scaR", [1, 16], F32)
    rsb = sbt("rsb", [128, 512], F32)
    sigr = sbt("sigr", [1, 512], F32)

    psA = nc.alloc_psum_tensor("psA", [128, 2048], F32)
    psU = [nc.alloc_psum_tensor(f"psU{i}", [128, 512], F32) for i in range(2)]
    psR = nc.alloc_psum_tensor("psR", [128, 512], F32)
    psB = nc.alloc_psum_tensor("psB", [128, 512], F32)
    ps8 = [psA[:, 512 * i:512 * i + 512] for i in range(4)] + \
          [psU[0][:, :], psU[1][:, :], psR[:, :], psB[:, :]]

    class _TTrack:
        def __init__(self, eng):
            self._e = eng
            self.last = None

        def matmul(self, *a, **k):
            self.last = self._e.matmul(*a, **k)
            return self.last

        def transpose(self, *a, **k):
            self.last = self._e.transpose(*a, **k)
            return self.last

        def __getattr__(self, n):
            return getattr(self._e, n)

    T = _TTrack(nc.tensor)

    class _VTrack:
        _OPS = ("tensor_tensor", "tensor_add", "tensor_sub", "tensor_mul",
                "tensor_max", "tensor_scalar", "tensor_single_scalar",
                "tensor_scalar_add", "tensor_scalar_sub", "tensor_scalar_mul",
                "tensor_scalar_max", "tensor_scalar_min", "tensor_copy",
                "tensor_reduce", "reduce_max", "reduce_sum", "reciprocal",
                "memset", "select", "transpose", "bn_stats", "bn_aggr")

        def __init__(self, eng):
            self._e = eng
            self.last = None

        def __getattr__(self, n):
            a = getattr(self._e, n)
            if n in self._OPS:
                def wrap(*args, **kw):
                    self.last = a(*args, **kw)
                    return self.last
                return wrap
            return a

    V = _VTrack(nc.vector)
    C, G, SY = nc.scalar, nc.gpsimd, nc.sync

    s_ld = Sem(nc, "s_ld")
    s_w = Sem(nc, "s_w")
    s_kv = Sem(nc, "s_kv")
    s_ag = Sem(nc, "s_ag")
    s_cc = Sem(nc, "s_cc")
    s_t = Sem(nc, "s_t")
    s_v = Sem(nc, "s_v")
    s_s = Sem(nc, "s_s")
    s_g = Sem(nc, "s_g")
    s_f = Sem(nc, "s_f")
    s_st = Sem(nc, "s_st")
    s_dq = Sem(nc, "s_dq")
    s_fin = Sem(nc, "s_fin")
    s_pr = Sem(nc, "s_pr")
    s_bt = Sem(nc, "s_bt")
    s_bs = Sem(nc, "s_bs")
    s_bv = Sem(nc, "s_bv")
    _BAR = [0]

    def barrier():
        b = Sem(nc, f"bar{_BAR[0]}")
        _BAR[0] += 1
        for eng in (T, V, C, G, SY):
            eng.sem_inc(b.h, 1)
        for eng in (T, V, C, G, SY):
            eng.wait_ge(b.h, 5)

    def dma(eng, out, in_, sem_obj, k=16):
        return sem_obj.inc(eng.dma_start(out=out, in_=in_), k)

    RG = [list(range(8))]

    def collective(kind, op, src, dst):
        i = G.collective_compute(kind, op, replica_groups=RG,
                                 ins=[src.ap().opt()], outs=[dst.ap().opt()])
        i.then_inc(s_cc.h, 1)
        s_cc.n += 1

    pid_sy = SY.partition_id()
    r_rke = SY.register("rke").__enter__()
    r_rko = SY.register("rko").__enter__()
    SY.reg_mov(r_rke, SY.compute_val(pid_sy - pid_sy % 2))
    SY.reg_mov(r_rko, SY.compute_val(pid_sy - pid_sy % 2 + 1))

    def rk_even_v():
        return SY.snap(r_rke, min_val=0, max_val=6)

    def rk_odd_v():
        return SY.snap(r_rko, min_val=1, max_val=7)

    with nc.Block() as blk0:
        @blk0.sync
        def _(sy):
            pass

    dma(SY, x_res[:, :], x_loc.rearrange("(t p) e -> p t e", p=128), s_ld)
    dma(SY, e0m[:, :], maskE0[:, :], s_ld)
    dma(SY, e2m[:, :], maskE2[:, :], s_ld)
    dma(SY, identS[:, :], identF[:, :], s_ld)
    dma(SY, biasin[:, :], b_in_pk[:, :], s_ld)
    dma(SY, bqS[:, :], bq_pk[:, :], s_ld)
    dma(SY, bkS[:, :], bk_pk[:, :], s_ld)
    dma(SY, brows_f[:, :], brows_d[:, :], s_ld)
    s_ld.wait(V)
    V.memset(onesb[:, :], 1.0)
    V.memset(onesf[:, :], 1.0)
    V.memset(macc[:, :], 0.0)
    V.memset(scaR[:, :], 0.0)
    V.memset(stats[:, 15:16], 1e-5)
    V.tensor_copy(brows[:, :], brows_f[:, :])
    s_v.inc(V.last)
    s_v.wait(T)
    s_v.wait(C)
    s_v.wait(SY)

    for i in range(16):
        dma(G, cpT_bf[1024 * i:1024 * i + 1024, :], w_cpT[1024 * i:1024 * i + 1024, :], s_g)
    for h in range(4):
        dma(G, bd2_bf[h], bd2_d[h], s_g)
    dma(G, bd1_bf[:, :, :, :], bd1_d[:, :, :, :], s_g)
    PRECAST_G = s_g.n

    def blocktree(view4, nblk32):
        pass

    def dequant_upgate(nat_d, trT_d, wslot):
        for j in range(3):
            s_v.wait(SY)
            dma(SY, wdf[:, :], nat_d[128 * j:128 * j + 128, :], s_dq)
            s_dq.wait(C)
            s_s.inc(C.activation(lnst[:, :], wdf[:, :], AF.Abs))
            s_s.wait(V)
            s_dq.wait(V)
            V.tensor_tensor(sqsc[:, 0:512], lnst[:, 0:1024:2], lnst[:, 1:1024:2], ALU.max)
            V.tensor_tensor(sqsc[:, 512:768], sqsc[:, 0:512:2], sqsc[:, 1:512:2], ALU.max)
            s_v.inc(V.tensor_tensor(sqsc[:, 768:896], sqsc[:, 512:768:2],
                                    sqsc[:, 513:768:2], ALU.max))
            s_v.wait(V)
            s_v.inc(V.tensor_tensor(sqsc[:, 896:960], sqsc[:, 768:896:2],
                                    sqsc[:, 769:896:2], ALU.max))
            s_v.wait(V)
            s_v.inc(V.tensor_tensor(sqsc[:, 960:992], sqsc[:, 896:960:2],
                                    sqsc[:, 897:960:2], ALU.max))
            s_v.wait(V)
            s_v.inc(V.tensor_scalar(sW[:, 0:32], sqsc[:, 960:992], 1.0 / 7.0, 1e-12,
                                    ALU.mult, ALU.max))
            s_v.wait(V)
            s_v.inc(V.reciprocal(sW[:, 32:64], sW[:, 0:32]))
            s_v.wait(V)
            w3 = wdf.rearrange("p (b e) -> p b e", b=32)
            V.tensor_tensor(lnst.rearrange("p (b e) -> p b e", b=32),
                            w3, sW[:, 32:64].unsqueeze(2).broadcast_to([128, 32, 32]),
                            ALU.mult)
            V.tensor_copy(wdi[:, 0:1024], lnst[:, :])
            s_v.inc(V.tensor_tensor(lnst.rearrange("p (b e) -> p b e", b=32),
                                    wdi[:, 0:1024].rearrange("p (b e) -> p b e", b=32),
                                    sW[:, 0:32].unsqueeze(2).broadcast_to([128, 32, 32]),
                                    ALU.mult))
            s_v.wait(T)
            for ec in range(8):
                T.transpose(psA[:, 128 * ec:128 * ec + 128],
                            lnst[:, 128 * ec:128 * ec + 128], identS[:, :])
            s_t.inc(T.last)
            s_t.wait(V)
            for ec in range(8):
                V.tensor_copy(wdqt[j % 2][:, 128 * ec:128 * ec + 128],
                              psA[:, 128 * ec:128 * ec + 128])
            s_v.inc(V.last)
            s_v.wait(SY)
            s_v.wait(T)
            dma(SY, vWin[wslot, j].rearrange("(c p) i -> p c i", p=128),
                wdqt[j % 2][:, :], s_st)

    dequant_upgate(wup_nat, wupT_sl, 0)
    dequant_upgate(wgate_nat, wgateT_sl, 1)

    for j in range(3):
        s_v.wait(SY)
        dma(SY, wdf[:, :],
            wdn_nat[:, 128 * j:128 * j + 128].rearrange("(c p) i -> p c i", p=128), s_dq)
        s_dq.wait(C)
        s_s.inc(C.activation(lnst[:, :], wdf[:, :], AF.Abs))
        s_s.wait(V)
        s_dq.wait(V)
        v4 = lnst.rearrange("p (c b i) -> p c b i", c=8, b=4)
        q4 = sqsc.rearrange("p (c b i) -> p c b i", c=8, b=4)
        V.tensor_tensor(q4[:, :, :, 0:16], v4[:, :, :, 0:32:2], v4[:, :, :, 1:32:2], ALU.max)
        s_v.inc(V.tensor_tensor(q4[:, :, :, 16:24], q4[:, :, :, 0:16:2],
                                q4[:, :, :, 1:16:2], ALU.max))
        s_v.wait(V)
        s_v.inc(V.tensor_tensor(q4[:, :, :, 24:28], q4[:, :, :, 16:24:2],
                                q4[:, :, :, 17:24:2], ALU.max))
        s_v.wait(V)
        s_v.inc(V.tensor_tensor(q4[:, :, :, 28:30], q4[:, :, :, 24:28:2],
                                q4[:, :, :, 25:28:2], ALU.max))
        s_v.wait(V)
        s_v.inc(V.tensor_tensor(q4[:, :, :, 30:31], q4[:, :, :, 28:30:2],
                                q4[:, :, :, 29:30:2], ALU.max))
        s_v.wait(V)
        s_v.inc(V.tensor_scalar(sW[:, 0:32].rearrange("p (c b) -> p c b", c=8),
                                q4[:, :, :, 30:31].squeeze(3), 1.0 / 7.0, 1e-12,
                                ALU.mult, ALU.max))
        s_v.wait(V)
        s_v.inc(V.reciprocal(sW[:, 32:64], sW[:, 0:32]))
        s_v.wait(V)
        sv = sW[:, 32:64].rearrange("p (c b) -> p c b", c=8).unsqueeze(3).broadcast_to([128, 8, 4, 32])
        ss = sW[:, 0:32].rearrange("p (c b) -> p c b", c=8).unsqueeze(3).broadcast_to([128, 8, 4, 32])
        w4 = wdf.rearrange("p (c b i) -> p c b i", c=8, b=4)
        V.tensor_tensor(lnst.rearrange("p (c b i) -> p c b i", c=8, b=4), w4, sv, ALU.mult)
        V.tensor_copy(wdi[:, 0:1024], lnst[:, :])
        s_v.inc(V.tensor_tensor(lnst.rearrange("p (c b i) -> p c b i", c=8, b=4),
                                wdi[:, 0:1024].rearrange("p (c b i) -> p c b i", c=8, b=4),
                                ss, ALU.mult))
        s_v.wait(T)
        for oc in range(8):
            T.transpose(psA[:, 128 * oc:128 * oc + 128],
                        lnst[:, 128 * oc:128 * oc + 128], identS[:, :])
        s_t.inc(T.last)
        s_t.wait(V)
        for oc in range(8):
            V.tensor_copy(wdqt[j % 2][:, 128 * oc:128 * oc + 128],
                          psA[:, 128 * oc:128 * oc + 128])
        s_v.inc(V.last)
        s_v.wait(SY)
        s_v.wait(T)
        dma(SY, vWinD[2, j], wdqt[j % 2][:, :], s_st)

    s_st.wait(G)
    s_g.wait(G)
    collective("AllGather", ALU.bypass, wdq_in, wdq_out)
    CC_AGW = s_cc.n
    barrier()

    def emit_ln(L):
        half_free = [None, None]
        stats_free = None
        for ti in range(NT):
            xt = x_res[:, 1024 * ti:1024 * ti + 1024]
            V.tensor_tensor(sqsc[:, :], xt, xt, ALU.mult)
            V.reduce_sum(stats[:, 0:1], sqsc[:, :], axis=AX.X)
            s_v.inc(V.reduce_sum(stats[:, 1:2], xt, axis=AX.X))
            s_v.wait(V)
            V.tensor_scalar_mul(stats[:, 2:3], stats[:, 1:2], 1.0 / 1024.0)
            s_v.inc(V.tensor_scalar_mul(stats[:, 3:4], stats[:, 0:1], 1.0 / 1024.0))
            s_v.wait(V)
            s_v.inc(V.tensor_tensor(stats[:, 4:5], stats[:, 2:3], stats[:, 2:3], ALU.mult))
            s_v.wait(V)
            s_v.inc(V.tensor_tensor(stats[:, 5:6], stats[:, 3:4], stats[:, 4:5], ALU.subtract))
            stats_free = s_v.n
            s_v.wait(C)
            s_s.inc(C.activation(stats[:, 6:7], stats[:, 5:6], AF.Sqrt, bias=stats[:, 15:16]))
            s_s.wait(V)
            s_v.inc(V.reciprocal(stats[:, 7:8], stats[:, 6:7]))
            s_v.wait(V)
            s_v.inc(V.tensor_tensor(stats[:, 8:9], stats[:, 2:3], stats[:, 7:8], ALU.mult))
            s_v.wait(V)
            if ti >= 1:
                s_t.wait(V)
            V.tensor_scalar(lnst[:, :], xt, stats[:, 7:8], stats[:, 8:9],
                            ALU.mult, ALU.subtract)
            if L == 0:
                s_v.inc(V.reduce_max(stats[:, 9:10], lnst[:, :], axis=AX.X,
                                     apply_absolute_value=True))
                s_v.wait(V)
                V.tensor_tensor(macc[:, 0:1], macc[:, 0:1], stats[:, 9:10], ALU.max)
            s_v.inc(V.last)
            half = 1024 * (ti % 2)
            s_v.wait(T)
            if half_free[ti % 2] is not None:
                s_v.wait(T, half_free[ti % 2][0])
            for ec in range(8):
                T.transpose(psA[:, half + 128 * ec:half + 128 * ec + 128],
                            lnst[:, 128 * ec:128 * ec + 128], identS[:, :])
            s_t.inc(T.last)
            s_t.wait(V)
            for ec in range(8):
                V.tensor_copy(xnq[:, 2048 * ec + 128 * ti:2048 * ec + 128 * ti + 128],
                              psA[:, half + 128 * ec:half + 128 * ec + 128])
            s_v.inc(V.last)
            half_free[ti % 2] = (s_v.n, s_v.n)

    def finalize_max(col, k, extra_wait=None):
        if extra_wait is not None:
            extra_wait()
        s_v.wait(T)
        s_g.wait(T)
        s_t.inc(T.transpose(psB[0:1, 0:128], macc[:, col:col + 1], identS[:, :]))
        s_t.wait(V)
        s_v.inc(V.reduce_max(scaR[0:1, 0:1], psB[0:1, 0:128], axis=AX.X))
        s_v.wait(T)
        s_v.wait(SY)
        dma(SY, ar_in[k][0:1, 0:1], scaR[0:1, 0:1], s_st)
        s_st.wait(G)
        collective("AllReduce", ALU.max, ar_in[k], ar_out[k])

    def broadcast_scale(k, qm=32767.0):
        s_cc.wait(SY)
        dma(SY, scaR[0:1, 2:3], ar_out[k][0:1, 0:1], s_ld)
        s_ld.wait(V)
        s_v.inc(V.tensor_scalar(scaR[0:1, 4:5], scaR[0:1, 2:3], 1.0 / qm, 1e-12,
                                ALU.mult, ALU.max))
        s_v.wait(V)
        s_v.inc(V.reciprocal(scaR[0:1, 5:6], scaR[0:1, 4:5]))
        s_v.wait(T)
        s_t.inc(T.matmul(psB[:, 0:2], onesf[0:1, :], scaR[0:1, 4:6], start=True, stop=True))
        s_t.wait(V)
        s_v.inc(V.tensor_copy(sca[:, 2 * k:2 * k + 2], psB[:, 0:2]))
        s_v.wait(V)
        s_v.wait(G)
        s_v.wait(T)

    def proj_fm(L, plan):
        wfree = [None, None]
        ufree = [None, None]
        ktfree = [None, None]
        s_cc.wait(G)
        s_st.wait(V)
        s_ag.wait(V)
        s_v.wait(T)
        for n, (kind, oc) in enumerate(plan):
            bufw = wst[n % 2]
            if L == 0:
                col = 128 * oc + (0 if kind == "q" else 1024)
                src = w_inT.rearrange("(c p) o -> p c o", p=128)[:, :, col:col + 128]
            else:
                wt = wqT_d if kind == "q" else wkT_d
                src = wt.rearrange("(c p) o -> p c o", p=128)[:, :, 128 * oc:128 * oc + 128]
            if wfree[n % 2] is not None:
                s_t.wait(G, wfree[n % 2])
            dma(G, bufw[:, :].rearrange("p (c i) -> p c i", c=8), src, s_w)
            s_w.wait(T)
            if kind == "k" and ktfree[n % 2] is not None:
                s_ag.wait(V, ktfree[n % 2])
            for g in range(4):
                pu = psU[g % 2]
                if ufree[g % 2] is not None:
                    s_v.wait(T, ufree[g % 2])
                for ec in range(8):
                    T.matmul(pu[:, :], bufw[:, 128 * ec:128 * ec + 128],
                             xnq[:, 2048 * ec + 512 * g:2048 * ec + 512 * g + 512],
                             start=(ec == 0), stop=(ec == 7))
                s_t.inc(T.last)
                s_t.wait(V)
                if kind == "q":
                    bcol = biasin[:, oc:oc + 1] if L == 0 else bqS[:, oc:oc + 1]
                    V.tensor_scalar(qT[:, 2048 * oc + 512 * g:2048 * oc + 512 * g + 512],
                                    pu[:, :], bcol, 0.125, ALU.add, ALU.mult)
                else:
                    bcol = biasin[:, 8 + oc:9 + oc] if L == 0 else bkS[:, oc:oc + 1]
                    V.tensor_scalar_add(ktst[n % 2][:, 512 * g:512 * g + 512], pu[:, :], bcol)
                s_v.inc(V.last)
                ufree[g % 2] = s_v.n
            wfree[n % 2] = s_t.n
            if kind == "k":
                s_v.wait(SY)
                dma(SY, agK_in[L][128 * oc:128 * oc + 128, :], ktst[n % 2][:, :], s_ag)
                ktfree[n % 2] = s_ag.n

    def emit_qkv(L):
        proj_fm(L, [("k", oc) for oc in range(8)])
        s_cc.wait(G)
        for vc in range(2):
            if L == 0:
                src = w_inT.rearrange("(c p) o -> p c o", p=128)[:, :, 2048 + 512 * vc:2560 + 512 * vc]
            else:
                src = wvT_d.rearrange("(c p) o -> p c o", p=128)[:, :, 512 * vc:512 * vc + 512]
            dma(G, wst2[vc][:, :].rearrange("p (c i) -> p c i", c=8), src, s_w)
        brow_ofs = 0 if L == 0 else 1024
        for i in range(2):
            V.memset(vst[i][:, :], 0.0)
            V.memset(vst[i][:, 64:V_ROW:65], 1.0)
        s_v.inc(V.last)
        s_w.wait(T)
        s_v.wait(T)
        vstfree = [None, None]
        pufree = [None, None]
        for ti in range(NT):
            if vstfree[ti % 2] is not None:
                s_ag.wait(V, vstfree[ti % 2])
            for vc in range(2):
                pu = psU[vc]
                if pufree[vc] is not None:
                    s_v.wait(T, pufree[vc])
                for ec in range(8):
                    T.matmul(pu[:, :], xnq[:, 2048 * ec + 128 * ti:2048 * ec + 128 * ti + 128],
                             wst2[vc][:, 512 * ec:512 * ec + 512],
                             start=(ec == 0), stop=False)
                s_t.inc(T.matmul(pu[:, :], onesb[0:1, :],
                                 brows[0:1, brow_ofs + 512 * vc:brow_ofs + 512 * vc + 512],
                                 start=False, stop=True))
                s_t.wait(V)
                dst = vst[ti % 2][:, 520 * vc:520 * vc + 520] \
                    .rearrange("p (h v) -> p h v", h=8)[:, :, 0:64]
                s_v.inc(V.tensor_copy(dst, pu[:, :].rearrange("p (h v) -> p h v", h=8)))
                pufree[vc] = s_v.n
            s_v.wait(SY)
            dma(SY, agV_in[L][128 * ti:128 * ti + 128, :], vst[ti % 2][:, :], s_ag)
            vstfree[ti % 2] = s_ag.n
        s_ag.wait(G)
        collective("AllGather", ALU.bypass, agK_in[L], agK_out[L])
        collective("AllGather", ALU.bypass, agV_in[L], agV_out[L])
        cc_kv = s_cc.n
        proj_fm(L, [("q", oc) for oc in range(8)])
        return cc_kv

    vlocv = vloc.rearrange("(t p) v -> p t v", p=128)

    def attn_rearrange(L):
        rke, rko = rk_even_v(), rk_odd_v()
        dma(SY, kKloc[:, 0:2048], agK_out[L][ds(rke, 1), :, :], s_kv)
        dma(SY, kKloc[:, 2048:4096], agK_out[L][ds(rko, 1), :, :], s_kv)
        dma(SY, vloc[0:2048, :], agV_out[L][ds(rke, 1), :, :], s_kv)
        dma(SY, vloc[2048:4096, :], agV_out[L][ds(rko, 1), :, :], s_kv)

    def attn_cache_loads(L, hp, buf):
        dma(SY, kcache[buf][:, :], kKloc[128 * hp:128 * hp + 128, :], s_kv)
        for hh in range(2):
            h = 2 * hp + hh
            dma(SY, vcache[buf][:, 2080 * hh:2080 * hh + 2080]
                .rearrange("p (t v) -> p t v", t=32),
                vlocv[:, :, 65 * h:65 * h + 65], s_kv)

    def attn_norm(hp, hh, pu, q0, qn, strided=False):
        s_bt.wait(V)
        V.reciprocal(sigr[0:1, 0:qn], pu[64:65, q0:q0 + qn])
        s_bv.inc(V.last)
        s_bv.wait(T)
        s_bt.inc(T.matmul(psR[:, 0:qn], onesf[0:1, :], sigr[0:1, 0:qn], start=True, stop=True))
        s_bt.wait(V)
        V.tensor_copy(rsb[0:64, 0:qn], psR[0:64, 0:qn])
        if strided:
            dst = xnq[64 * hh:64 * hh + 64, 2048 * hp + 32:2048 * hp + 2048:64]
        else:
            dst = xnq[64 * hh:64 * hh + 64, 2048 * hp + q0:2048 * hp + q0 + qn]
        s_bv.inc(V.tensor_tensor(dst, pu[0:64, q0:q0 + qn], rsb[0:64, 0:qn], ALU.mult))

    def attn0_body(p, hp, buf, kv_target):
        halfS = {0: None, 1: None}
        expSf = {0: None, 1: None}
        first = True
        for hh in range(2):
            r0 = 64 * hh
            for g in range(4):
                nch = L0_NCH[p][g]
                pu = psU[g % 2]
                nb = nch // 2
                for kb in range(nb + 1):
                    if kb < nb:
                        half = kb % 2
                        if halfS[half] is not None:
                            s_bs.wait(T, halfS[half])
                        if first:
                            s_kv.wait(T, kv_target)
                            first = False
                        if kb == 0 and (g >= 2 or hh == 1):
                            s_bv.wait(T)
                        for j in range(2):
                            cc = CC_OF(2 * kb + j)
                            T.matmul(psA[:, 1024 * half + 512 * j:1024 * half + 512 * j + 512],
                                     kcache[buf][r0:r0 + 64, 128 * cc:128 * cc + 128],
                                     qT[r0:r0 + 64, 2048 * hp + 512 * g:2048 * hp + 512 * g + 512],
                                     start=True, stop=True)
                        s_bt.inc(T.last)
                        nmask = sum(1 for j in range(2) if 2 * kb + j >= nch - 4)
                        if nmask:
                            s_bt.wait(V)
                            for j in range(2):
                                kc = 2 * kb + j
                                if kc >= nch - 4:
                                    jd = kc - (nch - 4)
                                    sl = psA[:, 1024 * half + 512 * j:1024 * half + 512 * j + 512]
                                    V.tensor_tensor(sl, sl,
                                                    e0m[:, 384 - 128 * jd:896 - 128 * jd], ALU.add)
                            s_bv.inc(V.last)
                            s_bv.wait(C)
                        if expSf[half] is not None:
                            s_bt.wait(C, expSf[half])
                        s_bt.wait(C)
                        s_bs.inc(C.activation(expS[half][:, :],
                                              psA[:, 1024 * half:1024 * half + 1024], AF.Exp))
                        halfS[half] = s_bs.n
                    if kb >= 1:
                        kbp = kb - 1
                        half = kbp % 2
                        s_bs.wait(T, halfS[half])
                        for j in range(2):
                            kc = 2 * kbp + j
                            cc = CC_OF(kc)
                            T.matmul(pu[0:65, :],
                                     vcache[buf][:, 2080 * hh + 65 * cc:2080 * hh + 65 * cc + 65],
                                     expS[half][:, 512 * j:512 * j + 512],
                                     start=(kc == 0), stop=(kc == nch - 1))
                        s_bt.inc(T.last)
                        expSf[half] = s_bt.n
                attn_norm(hp, hh, pu, 0, 512 * 0 + 512) if False else None
                attn_norm(hp, hh, pu, 0, 512) if False else None
                s_bt.wait(V)
                V.reciprocal(sigr[0:1, 0:512], pu[64:65, :])
                s_bv.inc(V.last)
                s_bv.wait(T)
                s_bt.inc(T.matmul(psR[:, :], onesf[0:1, :], sigr[0:1, 0:512],
                                  start=True, stop=True))
                s_bt.wait(V)
                V.tensor_copy(rsb[0:64, :], psR[0:64, :])
                dst = xnq[64 * hh:64 * hh + 64,
                          2048 * hp + 512 * g:2048 * hp + 512 * g + 512]
                s_bv.inc(V.tensor_tensor(dst, pu[0:64, :], rsb[0:64, :], ALU.mult))
        s_pr.inc(V.engine_nop())

    def attn1_body(p, hp, buf, kv_target):
        halfS = {0: None, 1: None}
        expSf = {0: None, 1: None}
        first = True
        for hh in range(2):
            r0 = 64 * hh
            for g in range(4):
                qb = GROUP_QB[p][g]
                pu = psU[g % 2]
                steps = []
                for r in range(6):
                    kc = qb // 128 - 1 + r
                    oob = kc < 0 or kc > 31
                    kce = min(max(kc, 0), 31)
                    w0, w1 = (0, 512) if oob else BAND_WIN[r]
                    steps.append((r, kce, oob, w0, w1))
                for rr in range(7):
                    if rr < 6:
                        r, kce, oob, w0, w1 = steps[rr]
                        wn = w1 - w0
                        half = rr % 2
                        if halfS[half] is not None:
                            s_bs.wait(T, halfS[half])
                        if first:
                            s_kv.wait(T, kv_target)
                            first = False
                        if rr == 0 and (g >= 2 or hh == 1):
                            s_bv.wait(T)
                        T.matmul(psA[:, 1024 * half:1024 * half + wn],
                                 kcache[buf][r0:r0 + 64, 128 * CC_OF(kce):128 * CC_OF(kce) + 128],
                                 qT[r0:r0 + 64,
                                    2048 * hp + 512 * g + w0:2048 * hp + 512 * g + w1],
                                 start=True, stop=True)
                        s_bt.inc(T.last)
                        s_bt.wait(V)
                        moff = 1152 if oob else (640 - 128 * r + w0)
                        sl = psA[:, 1024 * half:1024 * half + wn]
                        V.tensor_tensor(sl, sl, e2m[:, moff:moff + wn], ALU.add)
                        s_bv.inc(V.last)
                        s_bv.wait(C)
                        if expSf[half] is not None:
                            s_bt.wait(C, expSf[half])
                        s_bt.wait(C)
                        s_bs.inc(C.activation(expS[half][:, 0:wn], sl, AF.Exp))
                        halfS[half] = s_bs.n
                    if rr >= 1:
                        r, kce, oob, w0, w1 = steps[rr - 1]
                        wn = w1 - w0
                        half = (rr - 1) % 2
                        s_bs.wait(T, halfS[half])
                        T.matmul(pu[0:65, w0:w1],
                                 vcache[buf][:, 2080 * hh + 65 * CC_OF(kce):2080 * hh + 65 * CC_OF(kce) + 65],
                                 expS[half][:, 0:wn],
                                 start=(rr - 1 == 0), stop=(rr - 1 == 5))
                        s_bt.inc(T.last)
                        expSf[half] = s_bt.n
                s_bt.wait(V)
                V.reciprocal(sigr[0:1, 0:512], pu[64:65, :])
                s_bv.inc(V.last)
                s_bv.wait(T)
                s_bt.inc(T.matmul(psR[:, :], onesf[0:1, :], sigr[0:1, 0:512],
                                  start=True, stop=True))
                s_bt.wait(V)
                V.tensor_copy(rsb[0:64, :], psR[0:64, :])
                dst = xnq[64 * hh:64 * hh + 64,
                          2048 * hp + 512 * g:2048 * hp + 512 * g + 512]
                s_bv.inc(V.tensor_tensor(dst, pu[0:64, :], rsb[0:64, :], ALU.mult))
            pg = psB
            s_bv.wait(T)
            for kb in range(9):
                if kb < 8:
                    half = kb % 2
                    if halfS[half] is not None:
                        s_bs.wait(T, halfS[half])
                    for j in range(4):
                        kc = 4 * kb + j
                        T.matmul(psA[:, 1024 * half + 32 * j:1024 * half + 32 * j + 32],
                                 kcache[buf][r0:r0 + 64, 128 * kc:128 * kc + 128],
                                 qT[r0:r0 + 64, 2048 * hp + 32:2048 * hp + 2048:64],
                                 start=True, stop=True)
                    s_bt.inc(T.last)
                    if expSf[half] is not None:
                        s_bt.wait(C, expSf[half])
                    s_bt.wait(C)
                    s_bs.inc(C.activation(expS[half][:, 0:128],
                                          psA[:, 1024 * half:1024 * half + 128], AF.Exp))
                    halfS[half] = s_bs.n
                if kb >= 1:
                    kbp = kb - 1
                    half = kbp % 2
                    s_bs.wait(T, halfS[half])
                    for j in range(4):
                        kc = 4 * kbp + j
                        T.matmul(pg[0:65, 0:32],
                                 vcache[buf][:, 2080 * hh + 65 * kc:2080 * hh + 65 * kc + 65],
                                 expS[half][:, 32 * j:32 * j + 32],
                                 start=(kc == 0), stop=(kc == 31))
                    s_bt.inc(T.last)
                    expSf[half] = s_bt.n
            s_bt.wait(V)
            V.reciprocal(sigr[0:1, 0:32], pg[64:65, 0:32])
            s_bv.inc(V.last)
            s_bv.wait(T)
            s_bt.inc(T.matmul(psR[:, 0:32], onesf[0:1, :], sigr[0:1, 0:32],
                              start=True, stop=True))
            s_bt.wait(V)
            s_bv.inc(V.tensor_copy(rsb[0:64, 0:32], psR[0:64, 0:32]))
            s_bv.wait(V)
            dst = xnq[64 * hh:64 * hh + 64, 2048 * hp + 32:2048 * hp + 2048:64]
            s_bv.inc(V.tensor_tensor(dst, pg[0:64, 0:32], rsb[0:64, 0:32], ALU.mult))
        s_pr.inc(V.engine_nop())

    def emit_attn(L, cc_kv):
        body = attn0_body if L == 0 else attn1_body
        engs = (T._e, V._e, C, G, SY)
        regs = []
        for eng in engs:
            r = eng.register(f"parL{L}e{len(regs)}").__enter__()
            eng.reg_mov(r, eng.compute_val(eng.partition_id() % 2))
            regs.append(r)
        s_cc.wait(SY, cc_kv)
        s_st.wait(SY)
        s_ag.wait(SY)
        attn_rearrange(L)
        s_kv.wait(SY)
        snap = (s_bt.n, s_bs.n, s_bv.n, s_pr.n, s_kv.n)
        ends = []
        for par in nc.Switch(engines=list(engs),
                             index=[e.snap(r) for e, r in zip(engs, regs)], n=2):
            s_bt.n, s_bs.n, s_bv.n, s_pr.n, s_kv.n = snap
            pr_base = s_pr.n
            for hp in range(8):
                if hp >= 2:
                    s_pr.wait(SY, pr_base + hp - 1)
                attn_cache_loads(L, hp, hp % 2)
                body(par, hp, hp % 2, s_kv.n)
            ends.append((s_bt.n, s_bs.n, s_bv.n, s_pr.n, s_kv.n))
        assert ends[0] == ends[1], f"switch sem mismatch L{L}: {ends}"
        s_bt.n, s_bs.n, s_bv.n, s_pr.n, s_kv.n = ends[0]

    def emit_outproj(L):
        wt = w_outT if L == 0 else woT_d
        bofs = 3 * E if L == 0 else 2 * E
        s_cc.wait(G)
        for oc in range(2):
            src = wt.rearrange("(c p) o -> p c o", p=128)[:, :, 512 * oc:512 * oc + 512]
            dma(G, wst2[oc][:, :].rearrange("p (c i) -> p c i", c=8), src, s_w)
        s_w.wait(T)
        ufree = [None, None]
        for ti in range(NT):
            for oc in range(2):
                pu = psU[oc]
                if ufree[oc] is not None:
                    s_v.wait(T, ufree[oc])
                for ec in range(8):
                    T.matmul(pu[:, :], xnq[:, 2048 * ec + 128 * ti:2048 * ec + 128 * ti + 128],
                             wst2[oc][:, 512 * ec:512 * ec + 512],
                             start=(ec == 0), stop=False)
                s_t.inc(T.matmul(pu[:, :], onesb[0:1, :],
                                 brows[0:1, bofs + 512 * oc:bofs + 512 * oc + 512],
                                 start=False, stop=True))
                s_t.wait(V)
                col = 1024 * ti + 512 * oc
                s_v.inc(V.tensor_tensor(x_res[:, col:col + 512], x_res[:, col:col + 512],
                                        pu[:, :], ALU.add))
                ufree[oc] = s_v.n

    emit_ln(0)
    finalize_max(0, 0)
    barrier()
    CC_KV0 = emit_qkv(0)
    barrier()

    broadcast_scale(0)
    for ec in range(8):
        V.tensor_scalar_mul(ist[:, :], xnq[:, 2048 * ec:2048 * ec + 2048], sca[:, 1:2])
        s_v.inc(V.tensor_scalar_mul(xnq[:, 2048 * ec:2048 * ec + 2048],
                                    ist[:, :], sca[:, 0:1]))
    barrier()

    s_cc.wait(SY, CC_AGW)

    def upgate(wslot, dramT, maxcol):
        wfree = [None, None]
        slfree = [None, None, None, None]
        for ic in range(24):
            rk, j = ic // 3, ic % 3
            bufw = wst[ic % 2]
            if wfree[ic % 2] is not None:
                s_t.wait(SY, wfree[ic % 2])
            dma(SY, bufw[:, :].rearrange("p (c i) -> p c i", c=8),
                vWU[rk, wslot, j].rearrange("(c p) i -> p c i", p=128), s_w)
            s_w.wait(T)
            for g in range(4):
                sl = (ic * 4 + g) % 4
                pu = [psU[0], psU[1], psR, psB][(ic * 4 + g) % 4]
                if slfree[sl] is not None:
                    s_st.wait(T, slfree[sl][0])
                    s_v.wait(T, slfree[sl][1])
                for ec in range(8):
                    T.matmul(pu[:, :], bufw[:, 128 * ec:128 * ec + 128],
                             xnq[:, 2048 * ec + 512 * g:2048 * ec + 512 * g + 512],
                             start=(ec == 0), stop=(ec == 7))
                s_t.inc(T.last)
                s_t.wait(C)
                s_s.inc(C.copy(ugst[sl][:, :], pu[:, :]))
                s_s.wait(V)
                s_v.inc(V.reduce_max(stats[:, 11:12], ugst[sl][:, :], axis=AX.X,
                                     apply_absolute_value=True))
                s_v.wait(V)
                s_v.inc(V.tensor_tensor(macc[:, maxcol:maxcol + 1],
                                        macc[:, maxcol:maxcol + 1],
                                        stats[:, 11:12], ALU.max))
                mtick = s_v.n
                s_s.wait(SY)
                dma(SY, dramT[128 * ic:128 * ic + 128, 512 * g:512 * g + 512],
                    ugst[sl][:, :], s_st)
                slfree[sl] = (s_st.n, mtick)
            wfree[ic % 2] = s_t.n

    upgate(1, gateT_d, 1)
    finalize_max(1, 1)
    upgate(0, upT_d, 2)
    finalize_max(2, 2)
    barrier()

    emit_attn(0, CC_KV0)
    barrier()
    emit_outproj(0)
    barrier()

    broadcast_scale(1)
    s_st.wait(SY)
    bfree = [None, None]
    for it in range(96):
        ic, g = it // 4, it % 4
        b0, b1 = fst[it % 2], fst[2 + it % 2]
        if bfree[it % 2] is not None:
            s_st.wait(SY, bfree[it % 2])
        dma(SY, b0[:, :], gateT_d[128 * ic:128 * ic + 128, 512 * g:512 * g + 512], s_f)
        s_f.wait(V)
        V.tensor_scalar_mul(fst[6][:, :], b0[:, :], sca[:, 3:4])
        V.tensor_copy(wdi[:, 0:512], fst[6][:, :])
        s_v.inc(V.tensor_scalar_mul(b0[:, :], wdi[:, 0:512], sca[:, 2:3]))
        s_v.wait(C)
        s_s.inc(C.activation(b1[:, :], b0[:, :], AF.Sigmoid))
        s_s.wait(G)
        G.tensor_scalar_mul(fst[7][:, :], b1[:, :], 65536.0)
        G.tensor_copy(wdi[:, 512:1024], fst[7][:, :])
        s_g.inc(G.tensor_scalar(b1[:, :], wdi[:, 512:1024], 65535.0, 1.0 / 65536.0,
                                ALU.min, ALU.mult))
        s_g.wait(V)
        V.tensor_tensor(b1[:, :], b0[:, :], b1[:, :], ALU.mult)
        s_v.inc(V.reduce_max(stats[:, 13:14], b1[:, :], axis=AX.X,
                             apply_absolute_value=True))
        s_v.wait(V)
        s_v.inc(V.tensor_tensor(macc[:, 3:4], macc[:, 3:4], stats[:, 13:14], ALU.max))
        s_v.wait(SY)
        dma(SY, t4T_d[128 * ic:128 * ic + 128, 512 * g:512 * g + 512], b1[:, :], s_st)
        bfree[it % 2] = s_st.n
    finalize_max(3, 3)
    s_v.wait(V) if False else None
    V.memset(macc[:, 3:4], 0.0)
    s_v.inc(V.last)

    broadcast_scale(2)
    broadcast_scale(3)
    s_st.wait(SY)
    cfree = [None, None]
    for it in range(96):
        ic, g = it // 4, it % 4
        b0, b1 = fst[it % 2], fst[2 + it % 2]
        if cfree[it % 2] is not None:
            s_st.wait(SY, cfree[it % 2])
        dma(SY, b0[:, :], t4T_d[128 * ic:128 * ic + 128, 512 * g:512 * g + 512], s_f)
        dma(SY, b1[:, :], upT_d[128 * ic:128 * ic + 128, 512 * g:512 * g + 512], s_f)
        s_f.wait(G)
        G.tensor_scalar_mul(fst[7][:, :], b0[:, :], sca[:, 7:8])
        G.tensor_copy(wdi[:, 512:1024], fst[7][:, :])
        s_g.inc(G.tensor_scalar_mul(b0[:, :], wdi[:, 512:1024], sca[:, 6:7]))
        s_f.wait(V)
        V.tensor_scalar_mul(fst[6][:, :], b1[:, :], sca[:, 5:6])
        V.tensor_copy(wdi[:, 0:512], fst[6][:, :])
        V.tensor_scalar_mul(b1[:, :], wdi[:, 0:512], sca[:, 4:5])
        s_g.wait(V)
        V.tensor_tensor(b1[:, :], b0[:, :], b1[:, :], ALU.mult)
        s_v.inc(V.reduce_max(stats[:, 14:15], b1[:, :], axis=AX.X,
                             apply_absolute_value=True))
        s_v.wait(V)
        s_v.inc(V.tensor_tensor(macc[:, 3:4], macc[:, 3:4], stats[:, 14:15], ALU.max))
        s_v.wait(SY)
        dma(SY, t5T_d[128 * ic:128 * ic + 128, 512 * g:512 * g + 512], b1[:, :], s_st)
        cfree[it % 2] = s_st.n
    finalize_max(3, 4)

    broadcast_scale(4)
    s_st.wait(SY)
    t5free = [None, None]
    obfree = [None, None]
    wdnfree = [None, None]
    for g in range(4):
        for ic in range(24):
            rk, j = ic // 3, ic % 3
            b0, ob, bw = fst[ic % 2], obf[ic % 2], wst[ic % 2]
            if t5free[ic % 2] is not None:
                s_v.wait(SY, t5free[ic % 2])
            dma(SY, b0[:, :], t5T_d[128 * ic:128 * ic + 128, 512 * g:512 * g + 512], s_f)
            if wdnfree[ic % 2] is not None:
                s_t.wait(SY, wdnfree[ic % 2])
            dma(SY, bw[:, :], vWD[rk, 2, j], s_w)
            if obfree[ic % 2] is not None:
                s_t.wait(V, obfree[ic % 2])
            s_f.wait(V)
            V.tensor_scalar_mul(fst[6][:, :], b0[:, :], sca[:, 9:10])
            V.tensor_copy(wdi[:, 0:512], fst[6][:, :])
            s_v.inc(V.tensor_scalar_mul(ob[:, :], wdi[:, 0:512], sca[:, 8:9]))
            t5free[ic % 2] = s_v.n
            s_v.wait(T)
            s_w.wait(T)
            for tsub in range(4):
                for oc in range(2):
                    T.matmul(ps8[2 * tsub + oc], ob[:, 128 * tsub:128 * tsub + 128],
                             bw[:, 512 * oc:512 * oc + 512],
                             start=(ic == 0), stop=(ic == 23))
            s_t.inc(T.last)
            obfree[ic % 2] = s_t.n
            wdnfree[ic % 2] = s_t.n
        s_t.wait(V)
        for tsub in range(4):
            for oc in range(2):
                col = 1024 * (4 * g + tsub) + 512 * oc
                V.tensor_tensor(x_res[:, col:col + 512], x_res[:, col:col + 512],
                                ps8[2 * tsub + oc], ALU.add)
        s_v.inc(V.last)
        s_v.wait(T)
    barrier()

    if N_LAYERS > 1:
        emit_ln(1)
        barrier()
        CC_KV1 = emit_qkv(1)
        barrier()

        s_g.wait(SY, PRECAST_G)
        dma(SY, bd1_sb[:, :].rearrange("p (h c m) -> p h c m", h=4, c=8),
            bd1_bf.rearrange("h c p m -> p h c m"), s_w)
        s_w.wait(T)
        vbd2 = bd2_bf.rearrange("h a p m -> p h a m")
        hfree = [None, None]
        bdfree = [None, None]
        gelfree = [None, None]
        for g in range(4):
            for h in range(4):
                for ci in range(8):
                    it = (g * 4 + h) * 8 + ci
                    bb = bdst[it % 2]
                    if bdfree[it % 2] is not None:
                        s_t.wait(SY, bdfree[it % 2])
                    dma(SY, bb[:, :].rearrange("p (a m) -> p a m", a=4),
                        vbd2[:, h, 4 * ci:4 * ci + 4, :], s_w)
                    s_v.wait(T)
                    T.matmul(psB[:, :], bd1_sb[:, 1024 * h + 128 * ci:1024 * h + 128 * ci + 128],
                             xnq[:, 2048 * ci + 512 * g:2048 * ci + 512 * g + 512],
                             start=True, stop=True)
                    s_t.inc(T.last)
                    s_t.wait(V)
                    s_v.inc(V.tensor_copy(yst[:, :], psB[:, :]))
                    s_v.wait(T)
                    s_w.wait(T)
                    if gelfree[0] is not None:
                        s_s.wait(T, gelfree[0])
                    for az in range(4):
                        T.matmul(psA[:, 512 * az:512 * az + 512],
                                 bb[:, 128 * az:128 * az + 128], yst[:, :],
                                 start=True, stop=True)
                    s_t.inc(T.last)
                    s_t.wait(C)
                    if hfree[it % 2] is not None:
                        s_st.wait(C, hfree[it % 2])
                    C.activation(hst[it % 2][:, 0:1024], psA[:, 0:1024], AF.Gelu)
                    s_s.inc(C.activation(hst[it % 2][:, 1024:2048], psA[:, 1024:2048], AF.Gelu))
                    gelfree[0] = s_s.n
                    s_s.wait(SY)
                    A0 = 32 * h + 4 * ci
                    dma(SY, vH[:, A0:A0 + 4, 512 * g:512 * g + 512],
                        hst[it % 2][:, :].rearrange("p (a t) -> p a t", a=4), s_st)
                    hfree[it % 2] = s_st.n
                    bdfree[it % 2] = s_t.n
        barrier()

        s_st.wait(SY)
        hsfree = [None, None]
        cpfree = [None, None]
        for g in range(4):
            for ic in range(128):
                if hsfree[ic % 2] is not None:
                    s_t.wait(SY, hsfree[ic % 2])
                dma(SY, hst[ic % 2][:, 0:512],
                    h_dram[128 * ic:128 * ic + 128, 512 * g:512 * g + 512], s_f)
                if cpfree[ic % 2] is not None:
                    s_t.wait(SY, cpfree[ic % 2])
                dma(SY, cpst[ic % 2][:, :], cpT_bf[128 * ic:128 * ic + 128, :], s_w)
                s_f.wait(T)
                s_w.wait(T)
                for tsub in range(4):
                    for oc in range(2):
                        T.matmul(ps8[2 * tsub + oc],
                                 hst[ic % 2][:, 128 * tsub:128 * tsub + 128],
                                 cpst[ic % 2][:, 512 * oc:512 * oc + 512],
                                 start=(ic == 0), stop=(ic == 127))
                s_t.inc(T.last)
                hsfree[ic % 2] = s_t.n
                cpfree[ic % 2] = s_t.n
            s_t.wait(V)
            for tsub in range(4):
                for oc in range(2):
                    col = 1024 * (4 * g + tsub) + 512 * oc
                    V.tensor_tensor(x_res[:, col:col + 512], x_res[:, col:col + 512],
                                    ps8[2 * tsub + oc], ALU.add)
            s_v.inc(V.last)
            s_v.wait(T)
        barrier()

        emit_attn(1, CC_KV1)
        barrier()
        emit_outproj(1)
        barrier()

    if TAP is None:
        dma(SY, out_d.rearrange("(t p) e -> p t e", p=128), x_res[:, :], s_fin)
    elif TAP in ("xn0", "xq", "ctx0"):
        dma(G, out_d.rearrange("(a p) c -> p a c", p=128), xnq[:, :], s_fin)
    elif TAP == "q0":
        dma(G, out_d.rearrange("(a p) c -> p a c", p=128), qT[:, :], s_fin)
    s_fin.wait(SY)
    s_fin.wait(G)
    return nc


def kernel(**inputs):
    in_maps = _host_prepare(inputs)
    nc = bass.Bass("TRN2", target_bir_lowering=False)
    _build(nc)
    res = run_bass_kernel_spmd(nc, in_maps, core_ids=list(range(8)))
    return _assemble(res.results)
